# version 01
# baseline (speedup 1.0000x reference)
import numpy as np
import ml_dtypes
from contextlib import ExitStack
import concourse.bass as bass
import concourse.mybir as mybir

F32 = mybir.dt.float32
BF16 = mybir.dt.bfloat16
I32 = mybir.dt.int32
AF = mybir.ActivationFunctionType
OP = mybir.AluOpType
AX = mybir.AxisListType

D = 1024
H = 16
HD = 64
MLP = 4096
NEXP = 8
EPS = 1e-5
MEPS = 1e-6

ENGS = ["sync", "tensor", "vector", "scalar", "gpsimd"]


class KB:
    def __init__(self, nc, es):
        self.nc = nc
        self.ops = {e: [] for e in ENGS}
        self.sems = {e: es.enter_context(nc.semaphore("sem_" + e)) for e in ENGS}
        self.cnt = {e: 0 for e in ENGS}

    def _op(self, eng, emit, waits, dma):
        ws = []
        for w in (waits or []):
            if w is None:
                continue
            if isinstance(w, list):
                ws += [x for x in w if x is not None]
            else:
                ws.append(w)
        step = 16 if dma else 1
        self.cnt[eng] += step
        self.ops[eng].append((ws, emit, step))
        return (eng, self.cnt[eng])

    def t(self, emit, waits=None):
        return self._op("tensor", emit, waits, False)

    def v(self, emit, waits=None):
        return self._op("vector", emit, waits, False)

    def s(self, emit, waits=None):
        return self._op("scalar", emit, waits, False)

    def g(self, emit, waits=None):
        return self._op("gpsimd", emit, waits, False)

    def dsync(self, emit, waits=None):
        return self._op("sync", emit, waits, True)

    def dscal(self, emit, waits=None):
        return self._op("scalar", emit, waits, True)

    def dgp(self, emit, waits=None):
        return self._op("gpsimd", emit, waits, True)

    def now(self):
        return [(e, self.cnt[e]) for e in ENGS if self.cnt[e] > 0]

    def finalize(self, block, final_waits):
        self.ops["sync"].append(([w for w in final_waits if w], None, 0))
        for e in ENGS:
            ops = self.ops[e]
            sems = self.sems

            def body(engine, ops=ops, e=e):
                hwm = {s_: 0 for s_ in ENGS}
                issued = 0
                for waits, emitfn, step in ops:
                    if e == "gpsimd" and step == 16 and issued >= 96 * 16:
                        tgt = issued - 95 * 16
                        if tgt > hwm[e]:
                            engine.wait_ge(sems[e], tgt)
                            hwm[e] = tgt
                    for (se, val) in waits:
                        if val > hwm[se]:
                            engine.wait_ge(sems[se], val)
                            hwm[se] = val
                    if emitfn is not None:
                        emitfn(engine).then_inc(sems[e], step)
                        if step == 16:
                            issued += 16
            getattr(block, e)(body)


def host_constants():
    c = {}
    c["ident_bf"] = np.eye(128).astype(ml_dtypes.bfloat16)
    c["ones_col_f32"] = np.ones((1, 128), np.float32)
    c["ones_red_bf"] = np.ones((128, 1), ml_dtypes.bfloat16)
    r_of = (np.arange(128) // 16)
    m = (r_of[:, None] == r_of[None, :]).astype(np.float32)
    c["bd_mask"] = np.tile(m, (1, 4)).astype(ml_dtypes.bfloat16)
    return c


def col_form(vec, nch):
    return np.ascontiguousarray(np.asarray(vec, np.float32).reshape(nch, 128).T)


def build_launch_A(T, debug=False):
    TT = 512
    NTILE = T // TT
    NSLAB = TT // 128
    nc = bass.Bass("TRN2", target_bir_lowering=False)

    def inp(name, shape, dt=F32):
        return nc.dram_tensor(name, shape, dt, kind="ExternalInput").ap()

    x0_fm = inp("x0_fm", [D, T])
    wq_d = inp("wq", [D, D])
    wk_d = inp("wk", [D, D])
    wv_d = inp("wv", [D, D])
    wo_d = inp("wo", [D, D])
    fc_d = inp("fc_w", [D, MLP])
    pj_d = inp("proj_w", [MLP, D])
    wg_d = inp("wg", [D, NEXP])
    bq_c = inp("bq_c", [128, 8])
    bk_c = inp("bk_c", [128, 8])
    bo_c = inp("bo_c", [128, 8])
    fcb_c = inp("fcb_c", [128, 32])
    pjb_c = inp("pjb_c", [128, 8])
    ln1w_c = inp("ln1w_c", [128, 8])
    ln1b_c = inp("ln1b_c", [128, 8])
    ln2w_c = inp("ln2w_c", [128, 8])
    ln2b_c = inp("ln2b_c", [128, 8])
    bv_rep = inp("bv_rep", [128, D], BF16)
    ident_d = inp("ident_bf", [128, 128], BF16)
    onesc_d = inp("ones_col_f32", [1, 128])
    onesr_d = inp("ones_red_bf", [128, 1], BF16)
    mask_d = inp("bd_mask", [128, 512], BF16)

    x1_fm_o = nc.dram_tensor("x1_fm", [D, T], F32, kind="ExternalOutput").ap()
    lnx_fm_o = nc.dram_tensor("lnx_fm", [D, T], BF16, kind="ExternalOutput").ap()
    scores_o = nc.dram_tensor("scores", [T, NEXP], F32, kind="ExternalOutput").ap()
    dbg = {}
    if debug:
        for nm in ["xn", "g", "att"]:
            dbg[nm] = nc.dram_tensor("dbg_" + nm, [D, T], F32, kind="ExternalOutput").ap()
        dbg["xa"] = nc.dram_tensor("dbg_xa", [D, T], F32, kind="ExternalOutput").ap()

    xa_scr = nc.dram_tensor("xa_scr", [D, T], F32).ap()
    v_dr = nc.dram_tensor("v_dr", [128 * D], BF16).ap()
    a_dr = nc.dram_tensor("a_dr", [128 * 1024], BF16).ap()

    es = ExitStack()
    with es:
        def sb(name, shape, dt=F32):
            return es.enter_context(nc.sbuf_tensor(name, shape, dt))

        ident_s = sb("ident_s", [128, 128], BF16)
        onesc_s = sb("onesc_s", [1, 128], F32)
        onesr_s = sb("onesr_s", [128, 1], BF16)
        mask_s = sb("mask_s", [128, 512], BF16)
        bq_s = sb("bq_s", [128, 8])
        bk_s = sb("bk_s", [128, 8])
        bo_s = sb("bo_s", [128, 8])
        ln1w_s = sb("ln1w_s", [128, 8])
        ln1b_s = sb("ln1b_s", [128, 8])
        ln2w_s = sb("ln2w_s", [128, 8])
        ln2b_s = sb("ln2b_s", [128, 8])
        fcb_s = sb("fcb_s", [128, 32])
        pjb_s = sb("pjb_s", [128, 8])
        bv_s = sb("bv_s", [128, D], BF16)
        wg_s = sb("wg_s", [128, 8 * NEXP], BF16)
        rows = {k: sb("row_" + k, [1, TT]) for k in
                ["m", "q2", "v0", "vp", "sd", "ag", "vg", "vge", "sg", "tmp", "al"]}
        u_scr = sb("u_scr", [128, TT])
        e_scr = sb("e_scr", [128, TT])
        r_scr = sb("r_scr", [128, TT])
        w_scr = sb("w_scr", [128, TT])
        cast_ch = sb("cast_ch", [128, TT], BF16)

        pb = [es.enter_context(nc.psum_tensor("pb%d" % i, [128, 512], F32)) for i in range(7)]
        pt_bf = es.enter_context(nc.psum_tensor("pt_bf", [128, 1024], BF16))

        kb = KB(nc, es)

        cl = None
        for dst, src in [(ident_s, ident_d), (onesc_s, onesc_d), (onesr_s, onesr_d),
                         (mask_s, mask_d), (bq_s, bq_c), (bk_s, bk_c), (bo_s, bo_c),
                         (ln1w_s, ln1w_c), (ln1b_s, ln1b_c), (ln2w_s, ln2w_c),
                         (ln2b_s, ln2b_c), (fcb_s, fcb_c), (pjb_s, pjb_c),
                         (bv_s, bv_rep)]:
            cl = kb.dsync(lambda e, dst=dst, src=src: e.dma_start(out=dst[:, :], in_=src))
        wg_ld = kb.dgp(lambda e: e.dma_start(
            out=wg_s.rearrange("p (c j) -> p c j", j=NEXP),
            in_=wg_d.rearrange("(c p) j -> p c j", p=128)))

        with ExitStack() as es1:
            def sb1(name, shape, dt=F32):
                return es1.enter_context(nc.sbuf_tensor(name, shape, dt))

            wq_s = sb1("wq_s", [128, 8 * D], BF16)
            wk_s = sb1("wk_s", [128, 8 * D], BF16)
            wv_s = sb1("wv_s", [128, 8 * D], BF16)
            wo_s = sb1("wo_s", [128, 8 * D], BF16)
            x0t = sb1("x0t", [128, 8 * TT])
            g_sb = sb1("g_sb", [128, 8 * TT])
            xn_sb = sb1("xn_sb", [128, 8 * TT], BF16)
            fq_ch = sb1("fq_ch", [128, TT], BF16)
            fk_ch = sb1("fk_ch", [128, TT], BF16)
            v_tm = sb1("v_tm", [128, NSLAB * D], BF16)
            fqT = sb1("fqT", [64, H * TT], BF16)
            fkT = sb1("fkT", [64, H * TT], BF16)
            fkG = sb1("fkG", [64, 16 * 128], BF16)
            vstk = sb1("vstk", [128, 16 * 65], BF16)
            A_sb = sb1("A_sb", [128, 8 * 128], BF16)
            qk_scr = sb1("qk_scr", [128, 4])
            rec_scr = sb1("rec_scr", [128, 4])
            att_stk = sb1("att_stk", [128, 16 * 64], BF16)
            att_tm = sb1("att_tm", [128, D], BF16)
            att_fm = sb1("att_fm", [128, 8 * TT], BF16)

            wl = None
            for dst, src in [(wq_s, wq_d), (wk_s, wk_d), (wv_s, wv_d), (wo_s, wo_d)]:
                wl = kb.dgp(lambda e, dst=dst, src=src: e.dma_start(
                    out=dst.rearrange("p (c j) -> p c j", j=D),
                    in_=src.rearrange("(c p) j -> p c j", p=128)))
            vstk_init = kb.v(lambda e: e.memset(vstk[:, :], 1.0))

            prev_tile = []
            for it in range(NTILE):
                t0 = it * TT

                x0_ld = kb.dsync(lambda e, t0=t0: e.dma_start(
                    out=x0t.rearrange("p (c t) -> p c t", t=TT),
                    in_=x0_fm.rearrange("(c p) t -> p c t", p=128)[:, :, t0:t0 + TT]),
                    waits=prev_tile)

                s1 = s2 = None
                cst = None
                for c in range(8):
                    cst = kb.v(lambda e, c=c: e.tensor_copy(
                        cast_ch[:, :], x0t[:, c * TT:(c + 1) * TT]),
                        waits=[x0_ld, s1])
                    s1 = kb.t(lambda e, c=c: e.matmul(
                        pb[0][0:1, 0:TT], onesr_s[:, 0:1], cast_ch[:, :],
                        start=(c == 0), stop=(c == 7)), waits=[cst])
                sqc = None
                for c in range(8):
                    sqc = kb.s(lambda e, c=c: e.activation(
                        cast_ch[:, :], x0t[:, c * TT:(c + 1) * TT], AF.Square,
                        bias=0.0, scale=1.0), waits=[s1 if c == 0 else s2])
                    s2 = kb.t(lambda e, c=c: e.matmul(
                        pb[1][0:1, 0:TT], onesr_s[:, 0:1], cast_ch[:, :],
                        start=(c == 0), stop=(c == 7)), waits=[sqc])

                w = kb.v(lambda e: e.tensor_scalar_mul(rows["m"][:, :], pb[0][0:1, 0:TT], 1.0 / D), waits=[s1])
                w = kb.v(lambda e: e.tensor_scalar_mul(rows["q2"][:, :], pb[1][0:1, 0:TT], 1.0 / D), waits=[s2, w])
                w = kb.v(lambda e: e.tensor_mul(rows["tmp"][:, :], rows["m"][:, :], rows["m"][:, :]), waits=[w])
                w = kb.v(lambda e: e.tensor_sub(rows["v0"][:, :], rows["q2"][:, :], rows["tmp"][:, :]), waits=[w])
                vp = kb.v(lambda e: e.tensor_scalar_add(rows["vp"][:, :], rows["v0"][:, :], EPS), waits=[w])
                sd = kb.s(lambda e: e.activation(rows["sd"][:, :], rows["vp"][:, :], AF.Sqrt, bias=0.0, scale=1.0), waits=[vp])
                ag = kb.v(lambda e: e.reciprocal(rows["ag"][:, :], rows["sd"][:, :]), waits=[sd])
                w = kb.v(lambda e: e.reciprocal(rows["tmp"][:, :], rows["vp"][:, :]), waits=[vp, ag])
                w = kb.v(lambda e: e.tensor_mul(rows["vg"][:, :], rows["v0"][:, :], rows["tmp"][:, :]), waits=[w])
                vge = kb.v(lambda e: e.tensor_scalar_add(rows["vge"][:, :], rows["vg"][:, :], EPS), waits=[w])
                sg = kb.s(lambda e: e.activation(rows["sg"][:, :], rows["vge"][:, :], AF.Sqrt, bias=0.0, scale=1.0), waits=[vge])
                w = kb.v(lambda e: e.reciprocal(rows["tmp"][:, :], rows["sg"][:, :]), waits=[sg])
                al = kb.v(lambda e: e.tensor_mul(rows["al"][:, :], rows["ag"][:, :], rows["tmp"][:, :]), waits=[w])

                bc_m = kb.t(lambda e: e.matmul(pb[2][:, 0:TT], onesc_s[0:1, :], rows["m"][0:1, :], start=True, stop=True), waits=[al])
                bc_al = kb.t(lambda e: e.matmul(pb[3][:, 0:TT], onesc_s[0:1, :], rows["al"][0:1, :], start=True, stop=True), waits=[al])
                bc_ag = kb.t(lambda e: e.matmul(pb[4][:, 0:TT], onesc_s[0:1, :], rows["ag"][0:1, :], start=True, stop=True), waits=[al])

                last_xn = None
                for c in range(8):
                    sl = slice(c * TT, (c + 1) * TT)
                    w = kb.v(lambda e, sl=sl: e.tensor_sub(u_scr[:, :], x0t[:, sl], pb[2][:, 0:TT]),
                             waits=[bc_m, sqc, last_xn])
                    w = kb.v(lambda e, sl=sl: e.tensor_mul(g_sb[:, sl], u_scr[:, :], pb[4][:, 0:TT]),
                             waits=[w, bc_ag])
                    w = kb.v(lambda e: e.tensor_mul(w_scr[:, :], u_scr[:, :], pb[3][:, 0:TT]),
                             waits=[w, bc_al])
                    last_xn = kb.v(lambda e, c=c, sl=sl: e.tensor_scalar(
                        xn_sb[:, sl], w_scr[:, :], ln1w_s[:, c:c + 1], ln1b_s[:, c:c + 1],
                        OP.mult, OP.add), waits=[w])
                xn_all = last_xn

                if debug:
                    wdb = kb.dgp(lambda e, t0=t0: e.dma_start(
                        out=dbg["xn"].rearrange("(c p) t -> p c t", p=128)[:, :, t0:t0 + TT],
                        in_=xn_sb.rearrange("p (c t) -> p c t", t=TT)), waits=[xn_all])
                    pass

                qk_banks = [pb[5], pb[6]]
                bank_free = {5: None, 6: None}
                prev_exp = None
                fold_last = {"q": [], "k": []}
                for idx, (nm, w_s, bias_s, fch, dstT) in enumerate([
                        ("q", wq_s, bq_s, fq_ch, fqT), ("k", wk_s, bk_s, fk_ch, fkT)]):
                    for j in range(8):
                        bi = 5 + ((idx * 8 + j) % 2)
                        bank = pb[bi]
                        mm = None
                        for k in range(8):
                            mm = kb.t(lambda e, w_s=w_s, k=k, j=j, bank=bank: e.matmul(
                                bank[:, 0:TT],
                                w_s.rearrange("p (c j) -> p c j", j=D)[:, k, j * 128:(j + 1) * 128],
                                xn_sb[:, k * TT:(k + 1) * TT],
                                start=(k == 0), stop=(k == 7)),
                                waits=([wl, xn_all, bank_free[bi]] if k == 0 else None))
                        w1 = kb.v(lambda e, bias_s=bias_s, j=j, bank=bank: e.tensor_scalar(
                            u_scr[:, :], bank[:, 0:TT], bias_s[:, j:j + 1], 0.0,
                            OP.add, OP.min), waits=[mm, prev_exp])
                        w2 = kb.v(lambda e, bias_s=bias_s, j=j, bank=bank: e.tensor_scalar(
                            r_scr[:, :], bank[:, 0:TT], bias_s[:, j:j + 1], 0.0,
                            OP.add, OP.max), waits=[w1])
                        bank_free[bi] = w2
                        w3 = kb.s(lambda e: e.activation(e_scr[:, :], u_scr[:, :], AF.Exp,
                                                         bias=0.0, scale=1.0), waits=[w2])
                        prev_exp = w3
                        w4 = kb.v(lambda e, fch=fch: e.tensor_add(fch[:, :], e_scr[:, :], r_scr[:, :]),
                                  waits=[w3, fold_last[nm][-2:] if fold_last[nm] else None])
                        f1 = kb.dsync(lambda e, dstT=dstT, fch=fch, j=j: e.dma_start(
                            out=dstT[:, (2 * j) * TT:(2 * j + 1) * TT], in_=fch[0:64, :]),
                            waits=[w4])
                        f2 = kb.dsync(lambda e, dstT=dstT, fch=fch, j=j: e.dma_start(
                            out=dstT[:, (2 * j + 1) * TT:(2 * j + 2) * TT], in_=fch[64:128, :]),
                            waits=[w4])
                        fold_last[nm] += [f1, f2]
                folds_q = fold_last["q"][-2:]
                folds_k = fold_last["k"][-2:]


                v_banks_free = {2: None, 3: None}
                vslab_done = []
                for sslab in range(NSLAB):
                    vparts = []
                    for half in range(2):
                        bi = 2 + half
                        bank = pb[bi]
                        mm = None
                        for k in range(8):
                            mm = kb.t(lambda e, k=k, sslab=sslab, half=half, bank=bank: e.matmul(
                                bank[:, 0:TT],
                                xn_sb[:, k * TT + sslab * 128: k * TT + (sslab + 1) * 128],
                                wv_s.rearrange("p (c j) -> p c j", j=D)[:, k, half * 512:(half + 1) * 512],
                                start=(k == 0), stop=(k == 7)),
                                waits=([xn_all, wl, v_banks_free[bi]] if k == 0 else None))
                        w = kb.v(lambda e, sslab=sslab, half=half, bank=bank: e.scalar_tensor_tensor(
                            v_tm[:, sslab * D + half * 512: sslab * D + (half + 1) * 512],
                            bank[:, 0:TT], 0.0, bv_s[:, half * 512:(half + 1) * 512],
                            OP.add, OP.add), waits=[mm])
                        v_banks_free[bi] = w
                        vparts.append(w)
                    vslab_done.append(vparts[-1])

                st2_banks_free = {0: None, 1: None}
                st2_mm_last = {0: None, 1: None}
                prev_u1 = None
                prev_trs = None
                prev_st2_all = None
                prev_st1_all = None
                st1_last = None
                att_parts = []
                for sslab in range(NSLAB):
                    h1 = kb.dscal(lambda e, sslab=sslab: e.dma_start(
                        out=v_dr.rearrange("(p f) -> p f", p=128),
                        in_=v_tm[:, sslab * D:(sslab + 1) * D]),
                        waits=[vslab_done[sslab], vstk_init])
                    h2 = kb.dscal(lambda e: e.dma_start(
                        out=vstk.rearrange("p (g e) -> p g e", e=65)[:, :, 0:HD],
                        in_=v_dr.rearrange("(g m e) -> m g e", g=16, m=128, e=HD)),
                        waits=[h1, prev_st2_all])
                    vfold = h2
                    fkg_done = []
                    for gl in range(16):
                        g = sslab * 16 + gl

                        def mkc(g=g, gl=gl):
                            def emit(e):
                                return e.tensor_copy(
                                    fkG[:, gl * 128:(gl + 1) * 128],
                                    fkT.rearrange("p (h t) -> p t h", h=H)[:, g * 8:(g + 1) * 8, :])
                            return emit
                        wts = [folds_k, prev_st1_all]
                        if gl % 2 == 0:
                            fkg_done.append(kb.v(mkc(), waits=wts))
                        else:
                            fkg_done.append(kb.g(mkc(), waits=wts))
                    fkg_all = fkg_done[-2:]
                    st2_last = None
                    for q4 in range(4):
                        bi1 = 5 + (q4 % 2)
                        bank1 = pb[bi1]
                        slot = q4 % 2
                        mms = []
                        for gg in range(4):
                            g = sslab * 16 + q4 * 4 + gg
                            gl = q4 * 4 + gg
                            mms.append(kb.t(lambda e, g=g, gl=gl, gg=gg, bank1=bank1: e.matmul(
                                bank1[:, gg * 128:(gg + 1) * 128],
                                fkG[:, gl * 128:(gl + 1) * 128],
                                fqT.rearrange("p (h t) -> p t h", h=H)[:, g * 8:(g + 1) * 8, :],
                                start=(gg == 0), stop=(gg == 3)),
                                waits=([fkg_all, folds_q, bank_free[bi1]] if gg == 0 else None)))
                            st1_last = mms[-1]
                        ev = kb.v(lambda e, slot=slot, bank1=bank1: e.tensor_tensor(
                            A_sb[:, slot * 512:(slot + 1) * 512], bank1[:, :], mask_s[:, :], OP.mult),
                            waits=[mms[-1], st2_mm_last[slot]])
                        bank_free[bi1] = ev
                        bi = q4 % 2
                        bank = pb[bi]
                        mms = []
                        for gg in range(4):
                            g16 = q4 * 4 + gg
                            mms.append(kb.t(lambda e, g16=g16, gg=gg, slot=slot, bank=bank: e.matmul(
                                bank[:, gg * 65:(gg + 1) * 65],
                                A_sb[:, slot * 512 + gg * 128: slot * 512 + (gg + 1) * 128],
                                vstk[:, g16 * 65:(g16 + 1) * 65],
                                start=(gg == 0), stop=(gg == 3)),
                                waits=([ev, vfold, st2_banks_free[bi]] if gg == 0 else None)))
                        st2_mm_last[slot] = mms[-1]
                        st2_last = mms[-1]
                        wq_ = kb.v(lambda e, bank=bank: e.tensor_copy(
                            qk_scr[:, 0:4],
                            bank[:, 0:260].rearrange("p (g e) -> p g e", e=65)[:, :, 64:65]),
                            waits=[mms[-1]])
                        wr = kb.v(lambda e: e.tensor_scalar_add(rec_scr[:, 0:4], qk_scr[:, 0:4], MEPS), waits=[wq_])
                        wr = kb.v(lambda e: e.reciprocal(rec_scr[:, 0:4], rec_scr[:, 0:4]), waits=[wr])
                        evs = []
                        for gg in range(4):
                            g16 = q4 * 4 + gg
                            evs.append(kb.v(lambda e, gg=gg, g16=g16, bank=bank: e.tensor_scalar(
                                att_stk[:, g16 * 64:(g16 + 1) * 64],
                                bank[:, gg * 65:gg * 65 + 64],
                                rec_scr[:, gg:gg + 1], None, OP.mult),
                                waits=[wr, prev_u1]))
                        st2_banks_free[bi] = evs[-1]
                    prev_st2_all = st2_last
                    prev_st1_all = st1_last
                    u1 = kb.dscal(lambda e: e.dma_start(
                        out=a_dr.rearrange("(p f) -> p f", p=128), in_=att_stk[:, :]),
                        waits=[evs[-1], st2_banks_free[0], st2_banks_free[1]])
                    prev_u1 = u1
                    u2 = kb.dscal(lambda e: e.dma_start(
                        out=att_tm.rearrange("p (h e) -> p h e", e=HD),
                        in_=a_dr.rearrange("(m g e) -> g m e", m=128, g=16, e=HD)),
                        waits=[u1, prev_trs])
                    trs = []
                    for c in range(8):
                        trs.append(kb.t(lambda e, c=c: e.transpose(
                            pt_bf[:, c * 128:(c + 1) * 128],
                            att_tm[:, c * 128:(c + 1) * 128], ident_s[:, :]),
                            waits=([u2, att_parts[-1] if att_parts else None] if c == 0 else None)))
                    prev_trs = trs[-1]
                    ap_ = kb.v(lambda e, sslab=sslab: e.tensor_copy(
                        att_fm.rearrange("p (c t) -> p c t", t=TT)[:, :, sslab * 128:(sslab + 1) * 128],
                        pt_bf.rearrange("p (c t) -> p c t", t=128)), waits=[trs[-1]])
                    att_parts.append(ap_)

                if debug:
                    kb.dgp(lambda e, t0=t0: e.dma_start(
                        out=dbg["att"].rearrange("(c p) t -> p c t", p=128)[:, :, t0:t0 + TT],
                        in_=att_fm.rearrange("p (c t) -> p c t", t=TT)), waits=[att_parts[-1]])
                    kb.dgp(lambda e, t0=t0: e.dma_start(
                        out=dbg["g"].rearrange("(c p) t -> p c t", p=128)[:, :, t0:t0 + TT],
                        in_=g_sb.rearrange("p (c t) -> p c t", t=TT)), waits=[att_parts[-1]])

                xa_spills = []
                wo_banks_free = {2: None, 3: None}
                for j in range(8):
                    bi = 2 + (j % 2)
                    bank = pb[bi]
                    mm = None
                    for k in range(8):
                        mm = kb.t(lambda e, k=k, j=j, bank=bank: e.matmul(
                            bank[:, 0:TT],
                            wo_s.rearrange("p (c j) -> p c j", j=D)[:, k, j * 128:(j + 1) * 128],
                            att_fm[:, k * TT:(k + 1) * TT],
                            start=(k == 0), stop=(k == 7)),
                            waits=([att_parts[-1], wo_banks_free[bi],
                                    xa_spills[-1] if xa_spills else None] if k == 0 else None))
                    wx = kb.v(lambda e, j=j, bank=bank: e.scalar_tensor_tensor(
                        u_scr[:, :], bank[:, 0:TT], bo_s[:, j:j + 1],
                        g_sb[:, j * TT:(j + 1) * TT], OP.add, OP.add),
                        waits=[mm, xa_spills[-1] if xa_spills else None])
                    wo_banks_free[bi] = wx
                    sp = kb.dsync(lambda e, j=j, t0=t0: e.dma_start(
                        out=xa_scr.rearrange("(c p) t -> p c t", p=128)[:, j, t0:t0 + TT],
                        in_=u_scr[:, :]), waits=[wx])
                    xa_spills.append(sp)

                prev_tile = kb.now()

            a1_done = kb.now()

        with ExitStack() as es2:
            def sb2(name, shape, dt=F32):
                return es2.enter_context(nc.sbuf_tensor(name, shape, dt))

            xa_sb = sb2("xa_sb", [128, 8 * TT])
            xn2_sb = sb2("xn2_sb", [128, 8 * TT], BF16)
            h_sb = sb2("h_sb", [128, 32 * TT], BF16)
            x1_sb = sb2("x1_sb", [128, 8 * TT])
            lnx_sb = sb2("lnx_sb", [128, 8 * TT], BF16)
            x0_ch = sb2("x0_ch", [128, TT])
            fcblk = [sb2("fcblk%d" % i, [128, 8 * 128], BF16) for i in range(2)]
            pjblk = [sb2("pjblk%d" % i, [128, 32 * 128], BF16) for i in range(2)]
            sc_ps = sb2("sc_ps", [128, NEXP])
            sc_mx = sb2("sc_mx", [128, 1])
            sc_sum = sb2("sc_sum", [128, 1])

            def load_fc(j, buf, waits):
                return kb.dgp(lambda e, j=j, buf=buf: e.dma_start(
                    out=buf.rearrange("p (c j) -> p c j", j=128),
                    in_=fc_d.rearrange("(c p) j -> p c j", p=128)[:, :, j * 128:(j + 1) * 128]),
                    waits=waits)

            def load_pj(j, buf, waits):
                return kb.dgp(lambda e, j=j, buf=buf: e.dma_start(
                    out=buf.rearrange("p (c j) -> p c j", j=128),
                    in_=pj_d.rearrange("(c p) j -> p c j", p=128)[:, :, j * 128:(j + 1) * 128]),
                    waits=waits)

            blk_ld = {0: None, 1: None}
            blk_consumed = {0: None, 1: None}
            pj_ld = {0: None, 1: None}
            pj_consumed = {0: None, 1: None}

            prev_tile = list(a1_done)
            for it in range(NTILE):
                t0 = it * TT
                xa_ld = kb.dsync(lambda e, t0=t0: e.dma_start(
                    out=xa_sb.rearrange("p (c t) -> p c t", t=TT),
                    in_=xa_scr.rearrange("(c p) t -> p c t", p=128)[:, :, t0:t0 + TT]),
                    waits=prev_tile)

                s1 = s2 = None
                cst = None
                for c in range(8):
                    cst = kb.v(lambda e, c=c: e.tensor_copy(
                        cast_ch[:, :], xa_sb[:, c * TT:(c + 1) * TT]), waits=[xa_ld, s1])
                    s1 = kb.t(lambda e, c=c: e.matmul(
                        pb[0][0:1, 0:TT], onesr_s[:, 0:1], cast_ch[:, :],
                        start=(c == 0), stop=(c == 7)), waits=[cst])
                sqc = None
                for c in range(8):
                    sqc = kb.s(lambda e, c=c: e.activation(
                        cast_ch[:, :], xa_sb[:, c * TT:(c + 1) * TT], AF.Square,
                        bias=0.0, scale=1.0), waits=[s1 if c == 0 else s2])
                    s2 = kb.t(lambda e, c=c: e.matmul(
                        pb[1][0:1, 0:TT], onesr_s[:, 0:1], cast_ch[:, :],
                        start=(c == 0), stop=(c == 7)), waits=[sqc])
                w = kb.v(lambda e: e.tensor_scalar_mul(rows["m"][:, :], pb[0][0:1, 0:TT], 1.0 / D), waits=[s1])
                w = kb.v(lambda e: e.tensor_scalar_mul(rows["q2"][:, :], pb[1][0:1, 0:TT], 1.0 / D), waits=[s2, w])
                w = kb.v(lambda e: e.tensor_mul(rows["tmp"][:, :], rows["m"][:, :], rows["m"][:, :]), waits=[w])
                w = kb.v(lambda e: e.tensor_sub(rows["v0"][:, :], rows["q2"][:, :], rows["tmp"][:, :]), waits=[w])
                vp = kb.v(lambda e: e.tensor_scalar_add(rows["vp"][:, :], rows["v0"][:, :], EPS), waits=[w])
                sd = kb.s(lambda e: e.activation(rows["sd"][:, :], rows["vp"][:, :], AF.Sqrt, bias=0.0, scale=1.0), waits=[vp])
                a2 = kb.v(lambda e: e.reciprocal(rows["ag"][:, :], rows["sd"][:, :]), waits=[sd])
                bc_m = kb.t(lambda e: e.matmul(pb[2][:, 0:TT], onesc_s[0:1, :], rows["m"][0:1, :], start=True, stop=True), waits=[a2])
                bc_a2 = kb.t(lambda e: e.matmul(pb[3][:, 0:TT], onesc_s[0:1, :], rows["ag"][0:1, :], start=True, stop=True), waits=[a2])

                last_xn2 = None
                for c in range(8):
                    sl = slice(c * TT, (c + 1) * TT)
                    w = kb.v(lambda e, sl=sl: e.tensor_sub(u_scr[:, :], xa_sb[:, sl], pb[2][:, 0:TT]),
                             waits=[bc_m, sqc, last_xn2])
                    w = kb.v(lambda e: e.tensor_mul(w_scr[:, :], u_scr[:, :], pb[3][:, 0:TT]),
                             waits=[w, bc_a2])
                    last_xn2 = kb.v(lambda e, c=c, sl=sl: e.tensor_scalar(
                        xn2_sb[:, sl], w_scr[:, :], ln2w_s[:, c:c + 1], ln2b_s[:, c:c + 1],
                        OP.mult, OP.add), waits=[w])
                xn2_all = last_xn2

                gelu_done = []
                fcb_banks_free = {4: None, 5: None}
                blk_ld[0] = load_fc(0, fcblk[0], [blk_consumed[0]] + (a1_done if it == 0 else []))
                blk_ld[1] = load_fc(1, fcblk[1], [blk_consumed[1]] + (a1_done if it == 0 else []))
                for j in range(32):
                    bi2 = j % 2
                    buf = fcblk[bi2]
                    ld = blk_ld[bi2]
                    bi = 4 + (j % 2)
                    bank = pb[bi]
                    mm = None
                    for k in range(8):
                        mm = kb.t(lambda e, buf=buf, k=k, bank=bank: e.matmul(
                            bank[:, 0:TT],
                            buf.rearrange("p (c j) -> p c j", j=128)[:, k, :],
                            xn2_sb[:, k * TT:(k + 1) * TT],
                            start=(k == 0), stop=(k == 7)),
                            waits=([ld, xn2_all, fcb_banks_free[bi]] if k == 0 else None))
                    blk_consumed[bi2] = mm
                    nxt = j + 2
                    if nxt < 32:
                        blk_ld[bi2] = load_fc(nxt, fcblk[bi2], [blk_consumed[bi2]])
                    gd = kb.s(lambda e, j=j, bank=bank: e.activation(
                        h_sb[:, j * TT:(j + 1) * TT], bank[:, 0:TT], AF.Gelu,
                        bias=fcb_s[:, j:j + 1], scale=1.0), waits=[mm])
                    fcb_banks_free[bi] = gd
                    gelu_done.append(gd)
                h_all = gelu_done[-1]

                pj_banks_free = {6: None, 0: None}
                x1_parts = []
                pj_ld[0] = load_pj(0, pjblk[0], [pj_consumed[0], gelu_done[7]] + (a1_done if it == 0 else []))
                pj_ld[1] = load_pj(1, pjblk[1], [pj_consumed[1], gelu_done[7]] + (a1_done if it == 0 else []))
                for j in range(8):
                    bi2 = j % 2
                    buf = pjblk[bi2]
                    ld = pj_ld[bi2]
                    bi = 6 if (j % 2 == 0) else 0
                    bank = pb[bi]
                    mm = None
                    for k in range(32):
                        mm = kb.t(lambda e, buf=buf, k=k, bank=bank: e.matmul(
                            bank[:, 0:TT],
                            buf.rearrange("p (c j) -> p c j", j=128)[:, k, :],
                            h_sb[:, k * TT:(k + 1) * TT],
                            start=(k == 0), stop=(k == 31)),
                            waits=([ld, h_all, pj_banks_free[bi]] if k == 0 else None))
                    pj_consumed[bi2] = mm
                    nxt = j + 2
                    if nxt < 8:
                        pj_ld[bi2] = load_pj(nxt, pjblk[bi2], [pj_consumed[bi2]])
                    x0c = kb.dsync(lambda e, j=j, t0=t0: e.dma_start(
                        out=x0_ch[:, :],
                        in_=x0_fm.rearrange("(c p) t -> p c t", p=128)[:, j, t0:t0 + TT]),
                        waits=[x1_parts[-1] if x1_parts else xa_ld])
                    wx = kb.v(lambda e, j=j, bank=bank: e.scalar_tensor_tensor(
                        u_scr[:, :], bank[:, 0:TT], pjb_s[:, j:j + 1],
                        xa_sb[:, j * TT:(j + 1) * TT], OP.add, OP.add), waits=[mm])
                    pj_banks_free[bi] = wx
                    w2 = kb.v(lambda e, j=j: e.tensor_add(
                        x1_sb[:, j * TT:(j + 1) * TT], u_scr[:, :], x0_ch[:, :]),
                        waits=[wx, x0c])
                    x1_parts.append(w2)
                x1_all = x1_parts[-1]

                x1_st = kb.dsync(lambda e, t0=t0: e.dma_start(
                    out=x1_fm_o.rearrange("(c p) t -> p c t", p=128)[:, :, t0:t0 + TT],
                    in_=x1_sb.rearrange("p (c t) -> p c t", t=TT)), waits=[x1_all])

                s1 = s2 = None
                cst = None
                for c in range(8):
                    cst = kb.v(lambda e, c=c: e.tensor_copy(
                        cast_ch[:, :], x1_sb[:, c * TT:(c + 1) * TT]), waits=[x1_all, s1])
                    s1 = kb.t(lambda e, c=c: e.matmul(
                        pb[1][0:1, 0:TT], onesr_s[:, 0:1], cast_ch[:, :],
                        start=(c == 0), stop=(c == 7)), waits=[cst])
                sqc = None
                for c in range(8):
                    sqc = kb.s(lambda e, c=c: e.activation(
                        cast_ch[:, :], x1_sb[:, c * TT:(c + 1) * TT], AF.Square,
                        bias=0.0, scale=1.0), waits=[s1 if c == 0 else s2])
                    s2 = kb.t(lambda e, c=c: e.matmul(
                        pb[2][0:1, 0:TT], onesr_s[:, 0:1], cast_ch[:, :],
                        start=(c == 0), stop=(c == 7)), waits=[sqc])
                w = kb.v(lambda e: e.tensor_scalar_mul(rows["m"][:, :], pb[1][0:1, 0:TT], 1.0 / D), waits=[s1])
                w = kb.v(lambda e: e.tensor_scalar_mul(rows["q2"][:, :], pb[2][0:1, 0:TT], 1.0 / D), waits=[s2, w])
                w = kb.v(lambda e: e.tensor_mul(rows["tmp"][:, :], rows["m"][:, :], rows["m"][:, :]), waits=[w])
                w = kb.v(lambda e: e.tensor_sub(rows["v0"][:, :], rows["q2"][:, :], rows["tmp"][:, :]), waits=[w])
                vp = kb.v(lambda e: e.tensor_scalar_add(rows["vp"][:, :], rows["v0"][:, :], EPS), waits=[w])
                sd = kb.s(lambda e: e.activation(rows["sd"][:, :], rows["vp"][:, :], AF.Sqrt, bias=0.0, scale=1.0), waits=[vp])
                a3 = kb.v(lambda e: e.reciprocal(rows["ag"][:, :], rows["sd"][:, :]), waits=[sd])
                bc_m = kb.t(lambda e: e.matmul(pb[3][:, 0:TT], onesc_s[0:1, :], rows["m"][0:1, :], start=True, stop=True), waits=[a3])
                bc_a3 = kb.t(lambda e: e.matmul(pb[4][:, 0:TT], onesc_s[0:1, :], rows["ag"][0:1, :], start=True, stop=True), waits=[a3])
                last_lnx = None
                for c in range(8):
                    sl = slice(c * TT, (c + 1) * TT)
                    w = kb.v(lambda e, sl=sl: e.tensor_sub(u_scr[:, :], x1_sb[:, sl], pb[3][:, 0:TT]),
                             waits=[bc_m, sqc, last_lnx])
                    last_lnx = kb.v(lambda e, sl=sl: e.tensor_mul(lnx_sb[:, sl], u_scr[:, :], pb[4][:, 0:TT]),
                                    waits=[w, bc_a3])
                lnx_all = last_lnx
                lnx_st = kb.dsync(lambda e, t0=t0: e.dma_start(
                    out=lnx_fm_o.rearrange("(c p) t -> p c t", p=128)[:, :, t0:t0 + TT],
                    in_=lnx_sb.rearrange("p (c t) -> p c t", t=TT)), waits=[lnx_all])

                sc_stores = []
                for sslab in range(NSLAB):
                    mm = None
                    for k in range(8):
                        mm = kb.t(lambda e, k=k, sslab=sslab: e.matmul(
                            pb[5][:, 0:NEXP],
                            lnx_sb[:, k * TT + sslab * 128: k * TT + (sslab + 1) * 128],
                            wg_s.rearrange("p (c j) -> p c j", j=NEXP)[:, k, :],
                            start=(k == 0), stop=(k == 7)),
                            waits=([lnx_all, wg_ld, sc_stores[-1] if sc_stores else None] if k == 0 else None))
                    w = kb.v(lambda e: e.tensor_reduce(sc_mx[:, 0:1], pb[5][:, 0:NEXP], AX.X, OP.max, negate=True), waits=[mm])
                    w2 = kb.s(lambda e: e.activation(sc_ps[:, :], pb[5][:, 0:NEXP], AF.Exp,
                                                     bias=sc_mx[:, 0:1], scale=1.0), waits=[w])
                    w3 = kb.v(lambda e: e.tensor_reduce(sc_sum[:, 0:1], sc_ps[:, :], AX.X, OP.add), waits=[w2])
                    w4 = kb.v(lambda e: e.reciprocal(sc_sum[:, 0:1], sc_sum[:, 0:1]), waits=[w3])
                    w5 = kb.v(lambda e: e.tensor_scalar(sc_ps[:, :], sc_ps[:, :], sc_sum[:, 0:1], None, OP.mult),
                              waits=[w4])
                    stq = kb.dsync(lambda e, sslab=sslab, t0=t0: e.dma_start(
                        out=scores_o[t0 + sslab * 128: t0 + (sslab + 1) * 128, :],
                        in_=sc_ps[:, :]), waits=[w5])
                    sc_stores.append(stq)

                if debug:
                    kb.dsync(lambda e: e.dma_start(out=dbg["xa"], in_=xa_scr),
                             waits=[xa_ld])

                prev_tile = kb.now()

            block = es.enter_context(nc.Block())
            kb.finalize(block, kb.now())
    return nc


def host_inputs_A(core_x0_rows, weights):
    c = host_constants()
    T = core_x0_rows.shape[0]
    f = {
        "x0_fm": np.ascontiguousarray(core_x0_rows.T),
        "wq": weights["wq"], "wk": weights["wk"], "wv": weights["wv"], "wo": weights["wo"],
        "fc_w": weights["fc_w"], "proj_w": weights["proj_w"], "wg": weights["wg"],
        "bq_c": col_form(weights["bq"], 8), "bk_c": col_form(weights["bk"], 8),
        "bo_c": col_form(weights["bo"], 8), "fcb_c": col_form(weights["fc_b"], 32),
        "pjb_c": col_form(weights["proj_b"], 8),
        "ln1w_c": col_form(weights["ln1_w"], 8), "ln1b_c": col_form(weights["ln1_b"], 8),
        "ln2w_c": col_form(weights["ln2_w"], 8), "ln2b_c": col_form(weights["ln2_b"], 8),
        "bv_rep": np.ascontiguousarray(np.broadcast_to(np.asarray(weights["bv"]).astype(ml_dtypes.bfloat16), (128, D))),
        "ident_bf": c["ident_bf"], "ones_col_f32": c["ones_col_f32"],
        "ones_red_bf": c["ones_red_bf"], "bd_mask": c["bd_mask"],
    }
    return f


def host_constants_B():
    c = {}
    c["ident_bf"] = np.eye(128).astype(ml_dtypes.bfloat16)
    c["ident_f32"] = np.eye(128).astype(np.float32)
    c["ones_col_f32"] = np.ones((1, 128), np.float32)
    c["ones_red_bf"] = np.ones((128, 1), ml_dtypes.bfloat16)
    t = (np.arange(128)[:, None] < np.arange(128)[None, :]).astype(ml_dtypes.bfloat16)
    c["triL_bf"] = t
    return c


CAP = 640
NB_ITERS = 42
BIGIDX = 1.0e6


def build_launch_B(T):
    NSLAB2 = T // 128
    TALL = 16384
    nc = bass.Bass("TRN2", target_bir_lowering=False)

    def inp(name, shape, dt=F32):
        return nc.dram_tensor(name, shape, dt, kind="ExternalInput").ap()

    x1_rows = inp("x1_rows", [T, D])
    lnx_rows = inp("lnx_rows", [T, D], BF16)
    sc_all = inp("sc_all", [TALL, NEXP])
    sc_own = inp("sc_own", [T, NEXP])
    w1_d = inp("w1", [NEXP, D, 2 * D])
    w3_d = inp("w3", [NEXP, D, 2 * D])
    w2_d = inp("w2", [NEXP, 2 * D, D])
    ident_d = inp("ident_bf", [128, 128], BF16)
    identf_d = inp("ident_f32", [128, 128])
    onesc_d = inp("ones_col_f32", [1, 128])
    onesr_d = inp("ones_red_bf", [128, 1], BF16)
    tri_d = inp("triL_bf", [128, 128], BF16)

    out_o = nc.dram_tensor("out_rows", [T, D], F32, kind="ExternalOutput").ap()

    z_dr = nc.dram_tensor("z_dr", [NEXP * CAP, 1032], BF16).ap()
    oute_dr = nc.dram_tensor("oute_dr", [NEXP * CAP, D], F32).ap()

    es = ExitStack()
    with es:
        def sb(name, shape, dt=F32):
            return es.enter_context(nc.sbuf_tensor(name, shape, dt))

        ident_s = sb("ident_s", [128, 128], BF16)
        identf_s = sb("identf_s", [128, 128])
        onesc_s = sb("onesc_s", [1, 128], F32)
        onesr_s = sb("onesr_s", [128, 1], BF16)
        tri_s = sb("tri_s", [128, 128], BF16)
        scA = sb("scA", [128, 128 * NEXP])
        scO = sb("scO", [128, NSLAB2 * NEXP])
        lnx_sb = sb("lnx_sb", [128, NSLAB2 * 1032], BF16)
        idx_f = sb("idx_f", [128, 16])
        idx_i = sb("idx_i", [128, NEXP * 16], I32)
        msk_sb = sb("msk_sb", [128, 16], BF16)
        cmp_sb = sb("cmp_sb", [128, 128 * NEXP], BF16)
        thr_bc = sb("thr_bc", [128, NEXP])
        b_lo = sb("b_lo", [1, NEXP])
        b_hi = sb("b_hi", [1, NEXP])
        b_mid = sb("b_mid", [1, NEXP])
        b_cnt = sb("b_cnt", [1, NEXP])
        b_ge = sb("b_ge", [1, NEXP])
        b_t1 = sb("b_t1", [1, NEXP])
        b_row = sb("b_row", [1, 128 * NEXP])
        r_base = sb("r_base", [1, 16])
        r_cnt = sb("r_cnt", [1, 16])
        base_sb = sb("base_sb", [128, 16])
        scal1 = sb("scal1", [128, 1])
        zero_t = sb("zero_t", [128, 1032], BF16)
        z_sc = sb("z_sc", [128, 5 * 1032], BF16)
        zf = sb("zf", [128, 8 * CAP], BF16)
        h_sb2 = sb("h_sb2", [128, 16 * CAP], BF16)
        oute_fm = sb("oute_fm", [128, 8 * CAP])
        out_tm = sb("out_tm", [128, D])
        acc_sb = sb("acc_sb", [128, D])
        s_scr = sb("s_scr", [128, CAP])
        wblkA = [sb("wblkA%d" % i, [128, 8 * 128], BF16) for i in range(2)]
        wblkB = [sb("wblkB%d" % i, [128, 8 * 128], BF16) for i in range(2)]
        wblkC = [sb("wblkC%d" % i, [128, 16 * 128], BF16) for i in range(2)]

        pb = [es.enter_context(nc.psum_tensor("pb%d" % i, [128, 512], F32)) for i in range(7)]
        pt_bf = es.enter_context(nc.psum_tensor("pt_bf", [128, 1024], BF16))

        kb = KB(nc, es)

        cl = None
        for dst, src in [(ident_s, ident_d), (identf_s, identf_d), (onesc_s, onesc_d),
                         (onesr_s, onesr_d), (tri_s, tri_d)]:
            cl = kb.dsync(lambda e, dst=dst, src=src: e.dma_start(out=dst[:, :], in_=src))
        scA_ld = kb.dsync(lambda e: e.dma_start(
            out=scA.rearrange("p (j e) -> p j e", e=NEXP),
            in_=sc_all.rearrange("(p j) e -> p j e", p=128)))
        scO_ld = kb.dsync(lambda e: e.dma_start(
            out=scO.rearrange("p (s e) -> p s e", e=NEXP),
            in_=sc_own.rearrange("(s p) e -> p s e", p=128)))
        lnx_ld = None
        for s_ in range(NSLAB2):
            lnx_ld = kb.dsync(lambda e, s_=s_: e.dma_start(
                out=lnx_sb[:, s_ * 1032: s_ * 1032 + 1024],
                in_=lnx_rows[s_ * 128:(s_ + 1) * 128, :]), waits=None)
        scc = kb.v(lambda e: e.tensor_copy(
            lnx_sb.rearrange("p (s f) -> p s f", f=1032)[:, :, 1024:1032],
            scO.rearrange("p (s e) -> p s e", e=NEXP)), waits=[scO_ld])
        zt = kb.v(lambda e: e.memset(zero_t[:, :], 0.0))
        zz = None
        for r0 in range(0, NEXP * CAP, 128):
            zz = kb.dsync(lambda e, r0=r0: e.dma_start(
                out=z_dr[r0:r0 + 128, :], in_=zero_t[:, :]), waits=[zt])

        w = kb.v(lambda e: e.memset(b_lo[:, :], 0.0))
        w = kb.v(lambda e: e.memset(b_hi[:, :], 1.00001), waits=[w])
        bis_done = w
        for it in range(NB_ITERS):
            w = kb.v(lambda e: e.tensor_add(b_mid[:, :], b_lo[:, :], b_hi[:, :]), waits=[bis_done])
            w = kb.v(lambda e: e.tensor_scalar_mul(b_mid[:, :], b_mid[:, :], 0.5), waits=[w])
            mbc = kb.t(lambda e: e.matmul(pb[6][:, 0:NEXP], onesc_s[0:1, :], b_mid[0:1, :],
                                          start=True, stop=True), waits=[w])
            w = kb.v(lambda e: e.tensor_tensor(
                cmp_sb.rearrange("p (j e) -> p j e", e=NEXP),
                scA.rearrange("p (j e) -> p j e", e=NEXP),
                pb[6][:, 0:NEXP].rearrange("p e -> p () e").to_broadcast([128, 128, NEXP]),
                OP.is_ge), waits=[mbc, scA_ld])
            cnt_a = kb.t(lambda e: e.matmul(pb[5][0:1, 0:512], onesr_s[:, 0:1], cmp_sb[:, 0:512],
                                            start=True, stop=True), waits=[w])
            cnt_b = kb.t(lambda e: e.matmul(pb[2][0:1, 0:512], onesr_s[:, 0:1], cmp_sb[:, 512:1024],
                                            start=True, stop=True), waits=[w])
            w = kb.v(lambda e: e.tensor_copy(b_row[:, 0:512], pb[5][0:1, 0:512]), waits=[cnt_a])
            w = kb.v(lambda e: e.tensor_copy(b_row[:, 512:1024], pb[2][0:1, 0:512]), waits=[cnt_b, w])
            w = kb.v(lambda e: e.tensor_reduce(
                b_cnt[:, :], b_row.rearrange("p (j e) -> p e j", e=NEXP), AX.X, OP.add), waits=[w])
            w = kb.v(lambda e: e.tensor_scalar(b_ge[:, :], b_cnt[:, :], 4096.0, None, OP.is_ge), waits=[w])
            w = kb.v(lambda e: e.tensor_sub(b_t1[:, :], b_mid[:, :], b_lo[:, :]), waits=[w])
            w = kb.v(lambda e: e.tensor_mul(b_t1[:, :], b_t1[:, :], b_ge[:, :]), waits=[w])
            w = kb.v(lambda e: e.tensor_add(b_lo[:, :], b_lo[:, :], b_t1[:, :]), waits=[w])
            w = kb.v(lambda e: e.tensor_sub(b_t1[:, :], b_hi[:, :], b_mid[:, :]), waits=[w])
            w = kb.v(lambda e: e.tensor_mul(b_t1[:, :], b_t1[:, :], b_ge[:, :]), waits=[w])
            w = kb.v(lambda e: e.scalar_tensor_tensor(b_hi[:, :], b_t1[:, :], 0.0, b_mid[:, :],
                                                      OP.add, OP.add), waits=[w])
            bis_done = w
        thr_mm = kb.t(lambda e: e.matmul(pb[6][:, 0:NEXP], onesc_s[0:1, :], b_lo[0:1, :],
                                         start=True, stop=True), waits=[bis_done])
        thr_cp = kb.v(lambda e: e.tensor_copy(thr_bc[:, :], pb[6][:, 0:NEXP]), waits=[thr_mm])

        def load_w(which_d, e_, j, nch, buf, waits):
            return kb.dgp(lambda e, which_d=which_d, e_=e_, j=j, nch=nch, buf=buf: e.dma_start(
                out=buf.rearrange("p (c j) -> p c j", j=128),
                in_=which_d.rearrange("ex (c p) j -> ex p c j", p=128)[e_, :, :, j * 128:(j + 1) * 128]),
                waits=waits)

        wblk_free = {("A", 0): None, ("A", 1): None, ("B", 0): None, ("B", 1): None,
                     ("C", 0): None, ("C", 1): None}
        bankr = {0: thr_cp, 1: None, 2: thr_cp, 3: None}
        oute_done = []
        bch = {}

        def _mk_bc(e):
            bch["bc"] = e.alloc_register("bcidx")
            return e.reg_mov(bch["bc"], CAP - 1)
        kb.g(_mk_bc)
        scat_last = None
        for e_ in range(NEXP):
            mk = kb.v(lambda e, e_=e_: e.tensor_tensor(
                msk_sb[:, :],
                scO.rearrange("p (s e) -> p s e", e=NEXP)[:, :, e_],
                thr_bc[:, e_:e_ + 1].to_broadcast([128, 16]), OP.is_ge),
                waits=[thr_cp, scO_ld, scat_last])
            pfx = kb.t(lambda e: e.matmul(pb[4][:, 0:16], tri_s[:, :], msk_sb[:, :],
                                          start=True, stop=True), waits=[mk])
            csl = kb.t(lambda e: e.matmul(pb[5][0:1, 0:16], onesr_s[:, 0:1], msk_sb[:, :],
                                          start=True, stop=True), waits=[mk])
            w = kb.v(lambda e: e.tensor_copy(r_cnt[:, :], pb[5][0:1, 0:16]), waits=[csl])
            w = kb.v(lambda e: e.tensor_tensor_scan(r_cnt[:, :], r_cnt[:, :], r_cnt[:, :], 0.0,
                                                    OP.add, OP.bypass), waits=[w])
            w = kb.v(lambda e: e.memset(r_base[:, 0:1], 0.0), waits=[w])
            w = kb.v(lambda e: e.tensor_copy(r_base[:, 1:16], r_cnt[:, 0:15]), waits=[w])
            bmm = kb.t(lambda e: e.matmul(pb[3][:, 0:16], onesc_s[0:1, :], r_base[0:1, :],
                                          start=True, stop=True), waits=[w])
            w = kb.v(lambda e: e.tensor_copy(base_sb[:, :], pb[3][:, 0:16]), waits=[bmm])
            w = kb.v(lambda e: e.tensor_tensor(idx_f[:, :], pb[4][:, 0:16], base_sb[:, :], OP.add),
                     waits=[w, pfx])
            w2 = kb.v(lambda e: e.tensor_scalar(msk_sb[:, :], msk_sb[:, :], -1.0, -BIGIDX,
                                                OP.add, OP.mult), waits=[w])
            w2 = kb.v(lambda e: e.tensor_tensor(idx_f[:, :], idx_f[:, :], msk_sb[:, :], OP.add), waits=[w2])
            idxc = kb.v(lambda e, e_=e_: e.tensor_copy(idx_i[:, e_ * 16:(e_ + 1) * 16], idx_f[:, :]),
                        waits=[w2])
            scats = []
            for s_ in range(NSLAB2):
                sc_ = kb.dgp(lambda e, e_=e_, s_=s_: e.indirect_dma_start(
                    out=z_dr,
                    out_offset=bass.IndirectOffsetOnAxis(idx_i[:, e_ * 16 + s_: e_ * 16 + s_ + 1], 0),
                    in_=lnx_sb[:, s_ * 1032:(s_ + 1) * 1032],
                    in_offset=None,
                    element_offset=e_ * CAP * 1032,
                    bounds_check=bch["bc"], oob_is_err=False),
                    waits=[idxc, lnx_ld, scc, zz, scat_last])
                scats.append(sc_)
            scat_last = scats[-1]

            zld = kb.dsync(lambda e, e_=e_: e.dma_start(
                out=z_sc[:, :],
                in_=z_dr.rearrange("(ex s p) f -> ex p s f", ex=NEXP, p=128)[e_, :, :, :]),
                waits=[scat_last])
            ztr = []
            prev_ev = None
            for s_ in range(5):
                w = kb.v(lambda e, e_=e_, s_=s_: e.tensor_copy(
                    scal1[:, 0:1], z_sc[:, s_ * 1032 + 1024 + e_: s_ * 1032 + 1025 + e_]),
                    waits=[zld, prev_ev])
                w = kb.v(lambda e, s_=s_: e.tensor_scalar(
                    z_sc[:, s_ * 1032: s_ * 1032 + D], z_sc[:, s_ * 1032: s_ * 1032 + D],
                    scal1[:, 0:1], None, OP.mult), waits=[w])
                for c in range(8):
                    tr = kb.t(lambda e, s_=s_, c=c: e.transpose(
                        pt_bf[:, c * 128:(c + 1) * 128],
                        z_sc[:, s_ * 1032 + c * 128: s_ * 1032 + (c + 1) * 128], ident_s[:, :]),
                        waits=([w, prev_ev] if c == 0 else None))
                ev = kb.v(lambda e, s_=s_: e.tensor_copy(
                    zf.rearrange("p (c t) -> p c t", t=CAP)[:, :, s_ * 128:(s_ + 1) * 128],
                    pt_bf.rearrange("p (c t) -> p c t", t=128)), waits=[tr])
                prev_ev = ev
                ztr.append(ev)
            zf_all = ztr[-1]

            h_done = []
            for j in range(16):
                bi2 = j % 2
                lA = load_w(w1_d, e_, j, 8, wblkA[bi2], [wblk_free[("A", bi2)]])
                lB = load_w(w3_d, e_, j, 8, wblkB[bi2], [wblk_free[("B", bi2)]])
                mm1 = mm1b = mm3 = mm3b = None
                for k in range(8):
                    mm1 = kb.t(lambda e, j=j, k=k, bi2=bi2: e.matmul(
                        pb[0][:, 0:512],
                        wblkA[bi2].rearrange("p (c j) -> p c j", j=128)[:, k, :],
                        zf[:, k * CAP: k * CAP + 512],
                        start=(k == 0), stop=(k == 7)),
                        waits=([lA, zf_all, bankr[0]] if k == 0 else None))
                for k in range(8):
                    mm1b = kb.t(lambda e, j=j, k=k, bi2=bi2: e.matmul(
                        pb[1][:, 0:128],
                        wblkA[bi2].rearrange("p (c j) -> p c j", j=128)[:, k, :],
                        zf[:, k * CAP + 512: k * CAP + 640],
                        start=(k == 0), stop=(k == 7)),
                        waits=([bankr[1]] if k == 0 else None))
                for k in range(8):
                    mm3 = kb.t(lambda e, j=j, k=k, bi2=bi2: e.matmul(
                        pb[2][:, 0:512],
                        wblkB[bi2].rearrange("p (c j) -> p c j", j=128)[:, k, :],
                        zf[:, k * CAP: k * CAP + 512],
                        start=(k == 0), stop=(k == 7)),
                        waits=([lB, bankr[2]] if k == 0 else None))
                for k in range(8):
                    mm3b = kb.t(lambda e, j=j, k=k, bi2=bi2: e.matmul(
                        pb[3][:, 0:128],
                        wblkB[bi2].rearrange("p (c j) -> p c j", j=128)[:, k, :],
                        zf[:, k * CAP + 512: k * CAP + 640],
                        start=(k == 0), stop=(k == 7)),
                        waits=([bankr[3]] if k == 0 else None))
                wblk_free[("A", bi2)] = mm1b
                wblk_free[("B", bi2)] = mm3b
                sl1 = kb.s(lambda e: e.activation(s_scr[:, 0:512], pb[0][:, 0:512], AF.Silu,
                                                  bias=0.0, scale=1.0), waits=[mm1, mm3b])
                sl2 = kb.s(lambda e: e.activation(s_scr[:, 512:640], pb[1][:, 0:128], AF.Silu,
                                                  bias=0.0, scale=1.0), waits=[mm1b, sl1])
                hv1 = kb.v(lambda e, j=j: e.tensor_tensor(
                    h_sb2[:, j * CAP: j * CAP + 512], s_scr[:, 0:512], pb[2][:, 0:512], OP.mult),
                    waits=[sl1, mm3])
                hv2 = kb.v(lambda e, j=j: e.tensor_tensor(
                    h_sb2[:, j * CAP + 512: j * CAP + 640], s_scr[:, 512:640], pb[3][:, 0:128], OP.mult),
                    waits=[sl2, mm3b, hv1])
                bankr = {0: sl1, 1: sl2, 2: hv1, 3: hv2}
                h_done.append(hv2)
            h_all2 = h_done[-1]

            oute_evs = []
            for j in range(8):
                bi2 = j % 2
                lC = load_w(w2_d, e_, j, 16, wblkC[bi2], [wblk_free[("C", bi2)]])
                mmA = mmB = None
                for k in range(16):
                    mmA = kb.t(lambda e, j=j, k=k, bi2=bi2: e.matmul(
                        pb[4][:, 0:512],
                        wblkC[bi2].rearrange("p (c j) -> p c j", j=128)[:, k, :],
                        h_sb2[:, k * CAP: k * CAP + 512],
                        start=(k == 0), stop=(k == 15)),
                        waits=([lC, h_all2, oute_evs[-2] if len(oute_evs) >= 2 else None] if k == 0 else None))
                for k in range(16):
                    mmB = kb.t(lambda e, j=j, k=k, bi2=bi2: e.matmul(
                        pb[5][:, 0:128],
                        wblkC[bi2].rearrange("p (c j) -> p c j", j=128)[:, k, :],
                        h_sb2[:, k * CAP + 512: k * CAP + 640],
                        start=(k == 0), stop=(k == 15)),
                        waits=([oute_evs[-1] if oute_evs else None] if k == 0 else None))
                wblk_free[("C", bi2)] = mmB
                ev1 = kb.v(lambda e, j=j: e.tensor_copy(
                    oute_fm[:, j * CAP: j * CAP + 512], pb[4][:, 0:512]), waits=[mmA, mmB])
                ev2 = kb.v(lambda e, j=j: e.tensor_copy(
                    oute_fm[:, j * CAP + 512: j * CAP + 640], pb[5][:, 0:128]), waits=[ev1])
                oute_evs += [ev1, ev2]
            st_last = None
            for s_ in range(5):
                trs2 = []
                for c in range(8):
                    trs2.append(kb.t(lambda e, s_=s_, c=c: e.transpose(
                        pb[6][:, 0:128],
                        oute_fm.rearrange("p (c t) -> p c t", t=CAP)[:, c, s_ * 128:(s_ + 1) * 128],
                        identf_s[:, :]),
                        waits=[oute_evs[-1], st_last]))
                    w = kb.v(lambda e, c=c: e.tensor_copy(
                        out_tm[:, c * 128:(c + 1) * 128], pb[6][:, 0:128]), waits=[trs2[-1]])
                    st_last = w
                st_ = kb.dsync(lambda e, e_=e_, s_=s_: e.dma_start(
                    out=oute_dr[e_ * CAP + s_ * 128: e_ * CAP + (s_ + 1) * 128, :],
                    in_=out_tm[:, :]), waits=[st_last])
                st_last = st_
            oute_done.append(st_last)

        fin = []
        for s_ in range(NSLAB2):
            ld = kb.dsync(lambda e, s_=s_: e.dma_start(
                out=acc_sb[:, :], in_=x1_rows[s_ * 128:(s_ + 1) * 128, :]),
                waits=[fin[-1] if fin else None] + oute_done)
            gw = ld
            for e_ in range(NEXP):
                gw = kb.dgp(lambda e, e_=e_, s_=s_: e.indirect_dma_start(
                    out=acc_sb[:, :],
                    out_offset=None,
                    in_=oute_dr,
                    in_offset=bass.IndirectOffsetOnAxis(idx_i[:, e_ * 16 + s_: e_ * 16 + s_ + 1], 0),
                    element_offset=e_ * CAP * D,
                    bounds_check=bch["bc"], oob_is_err=False,
                    compute_op=OP.add),
                    waits=[gw])
            st_ = kb.dsync(lambda e, s_=s_: e.dma_start(
                out=out_o[s_ * 128:(s_ + 1) * 128, :], in_=acc_sb[:, :]), waits=[gw])
            fin.append(st_)

        block = es.enter_context(nc.Block())
        kb.finalize(block, kb.now())
    return nc


def host_inputs_B(x1_rows, lnx_rows, sc_all, sc_own, weights):
    c = host_constants_B()
    return {
        "x1_rows": np.ascontiguousarray(x1_rows, dtype=np.float32),
        "lnx_rows": np.ascontiguousarray(lnx_rows),
        "sc_all": np.ascontiguousarray(sc_all, dtype=np.float32),
        "sc_own": np.ascontiguousarray(sc_own, dtype=np.float32),
        "w1": weights["w1"], "w3": weights["w3"], "w2": weights["w2"],
        "ident_bf": c["ident_bf"], "ident_f32": c["ident_f32"],
        "ones_col_f32": c["ones_col_f32"], "ones_red_bf": c["ones_red_bf"],
        "triL_bf": c["triL_bf"],
    }


from concourse.bass_utils import run_bass_kernel_spmd

NCORES_K = 8
TOK_ALL = 16384
T_CORE = TOK_ALL // NCORES_K

_cache = {}


def kernel(**inputs):
    x0 = np.asarray(inputs["L0_attn_q_x"], np.float32).reshape(TOK_ALL, D)
    wA = {
        "wq": np.asarray(inputs["L0_attn_wq"], np.float32),
        "wk": np.asarray(inputs["L0_attn_wk"], np.float32),
        "wv": np.asarray(inputs["L0_attn_wv"], np.float32),
        "wo": np.asarray(inputs["L0_attn_wo"], np.float32),
        "bq": inputs["L0_attn_bq"], "bk": inputs["L0_attn_bk"],
        "bv": inputs["L0_attn_bv"], "bo": inputs["L0_attn_bo"],
        "ln1_w": inputs["L0_attn_ln1_w"], "ln1_b": inputs["L0_attn_ln1_b"],
        "ln2_w": inputs["L0_attn_ln2_w"], "ln2_b": inputs["L0_attn_ln2_b"],
        "fc_w": np.asarray(inputs["L0_attn_fc_w"], np.float32),
        "fc_b": inputs["L0_attn_fc_b"],
        "proj_w": np.asarray(inputs["L0_attn_proj_w"], np.float32),
        "proj_b": inputs["L0_attn_proj_b"],
        "wg": np.asarray(inputs["L0_ffn_wg"], np.float32),
    }
    if "A" not in _cache:
        _cache["A"] = build_launch_A(T_CORE)
    ncA = _cache["A"]
    feedsA = [host_inputs_A(x0[c * T_CORE:(c + 1) * T_CORE], wA) for c in range(NCORES_K)]
    resA = run_bass_kernel_spmd(ncA, feedsA, core_ids=list(range(NCORES_K)))

    sc_all = np.concatenate(
        [np.asarray(resA.results[c]["scores"], np.float32) for c in range(NCORES_K)], axis=0)
    x1_rows = [np.ascontiguousarray(np.asarray(resA.results[c]["x1_fm"], np.float32).T)
               for c in range(NCORES_K)]
    lnx_rows = [np.ascontiguousarray(np.asarray(resA.results[c]["lnx_fm"]).T)
                for c in range(NCORES_K)]

    wB = {"w1": np.asarray(inputs["L0_ffn_w1"], np.float32),
          "w3": np.asarray(inputs["L0_ffn_w3"], np.float32),
          "w2": np.asarray(inputs["L0_ffn_w2"], np.float32)}
    if "B" not in _cache:
        _cache["B"] = build_launch_B(T_CORE)
    ncB = _cache["B"]
    feedsB = [host_inputs_B(x1_rows[c], lnx_rows[c], sc_all,
                            sc_all[c * T_CORE:(c + 1) * T_CORE], wB)
              for c in range(NCORES_K)]
    resB = run_bass_kernel_spmd(ncB, feedsB, core_ids=list(range(NCORES_K)))

    out = np.concatenate(
        [np.asarray(resB.results[c]["out_rows"], np.float32) for c in range(NCORES_K)], axis=0)
    return out.reshape(2, 8192, D)
